# version 01 (final)
# baseline (speedup 1.0000x reference)
import contextlib
import numpy as np
import ml_dtypes
import concourse.bass as bass
import concourse.mybir as mybir
from concourse.bass_utils import run_bass_kernel_spmd

F32 = mybir.dt.float32
BF16 = mybir.dt.bfloat16
ACT = mybir.ActivationFunctionType
ALU = mybir.AluOpType
AXX = mybir.AxisListType

D = 2048
KG = 16
T = 1024
S2 = 512
H_REAL = 5461
HG = 43
HP = HG * 128
GCH = 4
NCH = KG // GCH
LVL = 10

A_EF, A_EC = 0, 8184
A_H = 16368
A_R = 24560
O_FA, O_CA, O_FB, O_CB = 32752, 33776, 34800, 35312
A_WBUF = 35824
A_P1 = 37872
A_SG = 39920
BC_A, BC_B = 40944, 41968
ROW_M, ROW_T = BC_B, BC_A
A_X0 = 0
XR_ST, X1_ST = 0, 2048
XQ1_0, XQ1_1 = 4096, 4608
HSQ_0, HSQ_1 = 32752, 33776
XB1_0, XB1_1 = 34800, 35312
B_G = 0
B_W1, B_W3 = 22016, 26112
B_T1, B_T3 = 30208, 31232
B_SG, B_C1 = 32256, 33280
B_QA, B_QB = 34304, 35328
B_X1L = 36352
B_W2 = 22016
B_T2 = 27520
B_QW = 33024
B_X1R = 35072
B_OUT = 36096
ARENA_N = 42992

EOFF = []
_o, _n = 0, T
for _ in range(LVL):
    EOFF.append(_o)
    _o += _n // 2
    _n //= 2
EPG = _o


class Sem:
    def __init__(self, stack, nc, name):
        self.h = stack.enter_context(nc.semaphore(name))
        self.n = 0

    def bump(self, k):
        self.n += k
        return self.n


def build_nc(debug_stage=None):
    nc = bass.Bass("TRN2", target_bir_lowering=False)
    stack = contextlib.ExitStack()

    d_x = nc.dram_tensor("xT", [D, T], F32, kind="ExternalInput")
    d_wr = nc.dram_tensor("wrs", [KG, 128, D], F32, kind="ExternalInput")
    d_wf = nc.dram_tensor("wfs", [KG, 128, D], F32, kind="ExternalInput")
    d_wc = nc.dram_tensor("wcs", [KG, 128, D], F32, kind="ExternalInput")
    d_wo = nc.dram_tensor("wos", [KG, 128, D], F32, kind="ExternalInput")
    d_w1 = nc.dram_tensor("w1s", [HG, 128, D], F32, kind="ExternalInput")
    d_w3 = nc.dram_tensor("w3s", [HG, 128, D], F32, kind="ExternalInput")
    d_w2 = nc.dram_tensor("w2s", [KG, 128, HP], F32, kind="ExternalInput")
    d_lns = nc.dram_tensor("lns", [128, KG], F32, kind="ExternalInput")
    d_gate = nc.dram_tensor("gates", [128, KG], F32, kind="ExternalInput")
    d_onesb = nc.dram_tensor("onesb", [128, 1], BF16, kind="ExternalInput")
    d_out = nc.dram_tensor("outT", [D, T], F32, kind="ExternalOutput")
    d_x1 = nc.dram_tensor("x1scr", [D, T], F32)
    d_ccin = nc.dram_tensor("ccin", [128, KG], F32)
    d_ccout = nc.dram_tensor("ccout", [256, KG], F32)
    d_rows = nc.dram_tensor("rowscr", [2, T], F32)
    d_srow = nc.dram_tensor("srowscr", [1, 8], F32)
    d_dbg = None
    if debug_stage is not None:
        d_dbg = nc.dram_tensor("dbg", [D, T], F32, kind="ExternalOutput")

    AR = stack.enter_context(nc.sbuf_tensor("AR", [128, ARENA_N], F32))
    XLN = stack.enter_context(nc.sbuf_tensor("XLN", [128, KG * T], BF16))
    t_lns = stack.enter_context(nc.sbuf_tensor("lns_t", [128, KG], F32))
    t_gate = stack.enter_context(nc.sbuf_tensor("gate_t", [128, KG], F32))
    t_onesb = stack.enter_context(nc.sbuf_tensor("onesb_t", [128, 1], BF16))
    t_K = stack.enter_context(nc.sbuf_tensor("K_t", [128, KG], F32))
    t_rootf = stack.enter_context(nc.sbuf_tensor("rootf", [128, KG], F32))
    t_rootc = stack.enter_context(nc.sbuf_tensor("rootc", [128, KG], F32))
    t_cst = stack.enter_context(nc.sbuf_tensor("cst", [128, 8], F32))
    t_bnd = stack.enter_context(nc.sbuf_tensor("bnd", [128, (LVL + 1) * KG], F32))
    t_acc = stack.enter_context(nc.sbuf_tensor("acc", [128, 308], F32))
    t_s = stack.enter_context(nc.sbuf_tensor("stile", [128, 8], F32))
    t_red = stack.enter_context(nc.sbuf_tensor("redt", [1, 8], F32))
    t_o128 = stack.enter_context(nc.sbuf_tensor("o128", [128, 128], F32))

    PS = {
        "PA": stack.enter_context(nc.psum_tensor("PA", [128, T], F32)),
        "PB": stack.enter_context(nc.psum_tensor("PB", [128, T], F32)),
        "PC": stack.enter_context(nc.psum_tensor("PC", [128, T], F32)),
        "PD": stack.enter_context(nc.psum_tensor("PD", [128, T], F32)),
    }
    PA, PB, PC, PD = PS["PA"], PS["PB"], PS["PC"], PS["PD"]

    ar_bf = AR[:, :].bitcast(BF16)

    def fv(off, n):
        return AR[:, off:off + n]

    def bvel(el_off, n):
        return ar_bf[:, el_off:el_off + n]

    def bview(slot_off, n):
        return ar_bf[:, 2 * slot_off:2 * slot_off + n]

    ops = {e: [] for e in ("sync", "gpsimd", "vector", "scalar", "tensor")}

    def emit(eng, fn):
        ops[eng].append(fn)

    def wait(eng, sem, val):
        if val is None:
            return
        ops[eng].append(lambda e, h=sem.h, v=val: e.wait_ge(h, v))

    S_X = Sem(stack, nc, "s_x")
    S_BC = Sem(stack, nc, "s_bc")
    S_CC = Sem(stack, nc, "s_cc")
    S_TE = Sem(stack, nc, "s_te")
    S_VE = Sem(stack, nc, "s_ve")
    S_SC = Sem(stack, nc, "s_sc")
    S_ST = Sem(stack, nc, "s_st")

    def dma(eng, out, in_, sem):
        v = sem.bump(16)
        emit(eng, lambda e, o=out, i=in_, h=sem.h: e.dma_start(out=o, in_=i).then_inc(h, 16))
        return v

    class DmaRing:
        def __init__(self, name, nslots):
            self.sems = [Sem(stack, nc, f"{name}{i}") for i in range(nslots)]
            self.nslots = nslots
            self.count = 0
            self.guards = []

        def issue(self, eng, out, in_):
            i = self.count
            self.count += 1
            if i >= self.nslots:
                g = self.guards[i - self.nslots]
                wait(eng, g[0], g[1])
            sem = self.sems[i % self.nslots]
            v = sem.bump(16)
            emit(eng, lambda e, o=out, ii=in_, h=sem.h: e.dma_start(out=o, in_=ii).then_inc(h, 16))
            self.guards.append(None)
            return i, sem, v

        def consumed(self, i, sem, val):
            self.guards[i] = (sem, val)

        def drain(self, eng):
            for s in self.sems:
                wait(eng, s, s.n)

    R_SLAB = DmaRing("r_sl", 2)
    R_P1 = DmaRing("r_p1", 2)
    R_W1 = DmaRing("r_w1", 2)
    R_W3 = DmaRing("r_w3", 2)
    R_XR = DmaRing("r_xr", 2)
    R_X1O = DmaRing("r_xo", 2)
    R_B0 = DmaRing("r_b0", 2)
    R_W2 = DmaRing("r_w2", 1)
    R_X1R = DmaRing("r_xr2", 1)

    def op(eng, fn, sem=None, k=1):
        if sem is None:
            emit(eng, fn)
            return None
        v = sem.bump(k)
        emit(eng, lambda e, f=fn, h=sem.h, kk=k: f(e).then_inc(h, kk))
        return v

    bank_guard = {b: None for b in PS}

    def guard_bank(eng, b):
        g = bank_guard[b]
        if g is not None:
            wait(eng, g[0], g[1])

    def set_bank_guard(b, sem, val):
        bank_guard[b] = (sem, val)

    def mm16(P, sl, mvv, sem_last):
        m = None
        for k in range(KG):
            last = (k == KG - 1)
            m = op("tensor", lambda e, k=k, P=P, sl=sl, mv=mvv: (
                e.matmul(P[:, 0:S2], sl[:, k * 128:(k + 1) * 128],
                         mv[:, k * T:k * T + S2], start=(k == 0), stop=(k == KG - 1)),
                e.matmul(P[:, S2:T], sl[:, k * 128:(k + 1) * 128],
                         mv[:, k * T + S2:(k + 1) * T], start=(k == 0), stop=(k == KG - 1)))[1],
                sem_last if last else None)
        return m

    def _memsets(e):
        e.memset(t_cst[:, 0:1], 1.0)
        e.memset(t_cst[:, 1:2], 1e-5)
        e.memset(t_cst[:, 2:3], -0.5)
        e.memset(t_cst[:, 3:4], 0.5)
        e.memset(t_cst[:, 4:5], 1.0 / 2048.0)
        e.memset(t_cst[:, 5:6], 1.0 / (2.0 * H_REAL * D))
        e.memset(t_cst[:, 6:7], -1.0)
        return e.memset(t_o128[:, :], 1.0)
    v_cst = op("vector", _memsets, S_VE)

    xv = fv(A_X0, KG * T)
    v_xld = None
    for g in range(KG):
        v_xld = dma("sync", xv[:, g * T:(g + 1) * T], d_x[g * 128:(g + 1) * 128, :], S_X)
    v_lns = dma("sync", t_lns[:, :], d_lns[:, :], S_BC)
    v_gate = dma("sync", t_gate[:, :], d_gate[:, :], S_BC)
    v_onesb = dma("gpsimd", t_onesb[:, :], d_onesb[:, :], S_BC)

    xb = bview(16384, KG * T)
    xsq = bview(24576, KG * T)
    wait("vector", S_X, v_xld)
    for g in range(KG):
        op("vector", lambda e, g=g: e.tensor_copy(xb[:, g * T:(g + 1) * T], xv[:, g * T:(g + 1) * T]))
    v_xb = op("vector", lambda e: e.tensor_tensor(xsq[:, :], xb[:, :], xb[:, :], ALU.mult), S_VE)

    if debug_stage == "xb":
        wait("gpsimd", S_VE, v_xb)
        for g in range(8):
            dma("gpsimd", d_dbg[g * 128:(g + 1) * 128, :], xb[:, g * T:(g + 1) * T], S_BC)
        for g in range(8):
            dma("gpsimd", d_dbg[(8 + g) * 128:(9 + g) * 128, :], xsq[:, g * T:(g + 1) * T], S_BC)
        wait("tensor", S_BC, S_BC.n)
    if debug_stage == "xbt":
        wait("gpsimd", S_VE, v_xb)
        for g in range(KG):
            dma("gpsimd", d_dbg[g * 128:(g + 1) * 128, :], xb[:, g * T:(g + 1) * T], S_BC)
        wait("tensor", S_BC, S_BC.n)

    wait("tensor", S_VE, v_xb)
    wait("tensor", S_BC, v_onesb)
    for g in range(KG):
        for s in range(2):
            op("tensor", lambda e, g=g, s=s: e.matmul(
                PA[0:1, s * S2:(s + 1) * S2], t_onesb[:, :],
                xb[:, g * T + s * S2:g * T + (s + 1) * S2],
                start=(g == 0), stop=(g == KG - 1)))
    t_stats1 = None
    for g in range(KG):
        for s in range(2):
            last = (g == KG - 1 and s == 1)
            t_stats1 = op("tensor", lambda e, g=g, s=s: e.matmul(
                PB[0:1, s * S2:(s + 1) * S2], t_onesb[:, :],
                xsq[:, g * T + s * S2:g * T + (s + 1) * S2],
                start=(g == 0), stop=(g == KG - 1)), S_TE if last else None)

    rm = AR[0:1, ROW_M:ROW_M + T]
    rt = AR[0:1, ROW_T:ROW_T + T]

    if debug_stage == "stats":
        sr = fv(A_SG, T)[0:1, :]
        wait("vector", S_TE, t_stats1)
        stv = op("vector", lambda e: e.tensor_copy(sr, PA[0:1, :]), S_VE)
        wait("gpsimd", S_VE, stv)
        dma("gpsimd", d_dbg[0:1, :], sr, S_BC)
        wait("vector", S_BC, S_BC.n)
        stv2 = op("vector", lambda e: e.tensor_copy(sr, PB[0:1, :]), S_VE)
        wait("gpsimd", S_VE, stv2)
        dma("gpsimd", d_dbg[1:2, :], sr, S_BC)
        wait("scalar", S_BC, S_BC.n)

    def emit_rows(sum_ps, sq_ps, te_guard):
        wait("scalar", S_TE, te_guard)
        sc_m = op("scalar", lambda e, P=sum_ps: e.mul(rm, P, t_cst[0:1, 4:5]), S_SC)
        wait("vector", S_SC, sc_m)
        op("vector", lambda e: e.tensor_tensor(rt, rm, rm, ALU.mult))
        ve_var = op("vector", lambda e, P=sq_ps: e.scalar_tensor_tensor(
            rt, P, t_cst[0:1, 4:5], rt, ALU.mult, ALU.subtract), S_VE)
        wait("scalar", S_VE, ve_var)
        op("scalar", lambda e: e.activation(rt, rt, ACT.Ln, bias=t_cst[0:1, 1:2]))
        sc_r = op("scalar", lambda e: e.activation(rt, rt, ACT.Exp, scale=t_cst[0:1, 2:3]), S_SC)
        wait("vector", S_SC, sc_r)
        ve_mr = op("vector", lambda e: e.tensor_tensor(rm, rm, rt, ALU.mult), S_VE)
        return ve_var, ve_mr

    wait("scalar", S_VE, v_cst)
    ve_var1, ve_rows1 = emit_rows(PA[0:1, :], PB[0:1, :], t_stats1)
    set_bank_guard("PA", S_VE, ve_var1)
    set_bank_guard("PB", S_VE, ve_var1)

    bc_a = fv(BC_A, T)
    bc_b = fv(BC_B, T)

    def emit_bcast(two):
        dma("gpsimd", d_rows[0:1, :], rt, S_BC)
        if two:
            dma("gpsimd", d_rows[1:2, :], rm, S_BC)
        wait("gpsimd", S_BC, S_BC.n)
        v2 = dma("gpsimd", bc_a, d_rows[0:1, :].partition_broadcast(128), S_BC)
        if two:
            v2 = dma("gpsimd", bc_b, d_rows[1:2, :].partition_broadcast(128), S_BC)
        return v2

    wait("gpsimd", S_VE, ve_rows1)
    g_bc1 = emit_bcast(True)

    sgT = fv(A_SG, T)
    wait("vector", S_BC, g_bc1)
    v_xln = None
    for g in range(KG):
        op("vector", lambda e, g=g: e.tensor_tensor(sgT, xv[:, g * T:(g + 1) * T], bc_a, ALU.mult))
        v_xln = op("vector", lambda e, g=g: e.tensor_tensor(
            XLN[:, g * T:(g + 1) * T], sgT, bc_b, ALU.subtract), S_VE)

    if debug_stage == "xln":
        wait("gpsimd", S_VE, v_xln)
        for g in range(KG):
            dma("gpsimd", d_dbg[g * 128:(g + 1) * 128, :], XLN[:, g * T:(g + 1) * T], S_BC)
    if debug_stage == "rows":
        wait("gpsimd", S_VE, v_xln)
        dma("gpsimd", d_dbg[0:128, :], bc_a, S_BC)
        dma("gpsimd", d_dbg[128:256, :], bc_b, S_BC)

    p1_done = {}

    def emit_p1_piece(src_ap, n, colx):
        slot = fv(A_P1 + (R_P1.count % 2) * 1024, n)
        i, psem, pv = R_P1.issue("sync", slot, src_ap)
        wait("scalar", psem, pv)
        p1_done[i] = op("scalar", lambda e, s=slot, c=colx: e.activation(
            s, s, ACT.Abs, accum_out=t_acc[:, c:c + 1]), S_SC)
        R_P1.consumed(i, S_SC, p1_done[i])

    col = 0
    for dt_ in (d_w1, d_w3):
        for hg in range(HG):
            for half in range(2):
                emit_p1_piece(dt_[hg, :, half * 1024:(half + 1) * 1024], 1024, col)
                col += 1
    P1_W1_LAST = p1_done[85]
    P1_W3_LAST = p1_done[171]

    def emit_s_finalize(c0, c1, scol, idx, sc_guard):
        rcol = 300 + idx
        wait("vector", S_SC, sc_guard)
        vv = op("vector", lambda e, a=c0, b=c1, r=rcol: e.tensor_reduce(
            t_acc[:, r:r + 1], t_acc[:, a:b], AXX.X, ALU.add), S_VE)
        wait("tensor", S_VE, vv)
        guard_bank("tensor", "PA")
        mm = op("tensor", lambda e, r=rcol: e.matmul(
            PA[:, 0:1], t_o128[:, :], t_acc[:, r:r + 1], start=True, stop=True), S_TE)
        wait("scalar", S_TE, mm)
        sv = op("scalar", lambda e, s=scol: e.mul(
            t_s[:, s:s + 1], PA[:, 0:1], t_cst[:, 5:6]), S_SC)
        set_bank_guard("PA", S_SC, sv)
        wait("vector", S_SC, sv)
        vn = op("vector", lambda e, s=scol: e.tensor_scalar(
            t_s[:, s + 3:s + 4], t_s[:, s:s + 1], t_cst[:, 6:7], None, ALU.mult), S_VE)
        return vn

    def emit_slab(src_ap):
        slot = bview(A_WBUF + (R_SLAB.count % 2) * 1024, 2048)
        i, ssem, sv = R_SLAB.issue("gpsimd", slot, src_ap)
        return i, slot, (ssem, sv)

    def slab_consumed(i, sem, val):
        R_SLAB.consumed(i, sem, val)

    fc_evac_f = {}
    fc_evac_c = {}
    up_done = {}

    for ch in range(NCH):
        for gi in range(GCH):
            og = ch * GCH + gi
            par = og % 2
            PF = PA if par == 0 else PC
            PFn = "PA" if par == 0 else "PC"
            PCm = PB if par == 0 else PD
            PCn = "PB" if par == 0 else "PD"
            i_f, slot_f, v_f = emit_slab(d_wf[og, :, :])
            i_c, slot_c, v_c = emit_slab(d_wc[og, :, :])
            wait("tensor", v_f[0], v_f[1])
            if og == 0:
                wait("tensor", S_VE, v_xln)
            guard_bank("tensor", PFn)
            te_f = mm16(PF, slot_f, XLN[:, :], S_TE)
            slab_consumed(i_f, S_TE, te_f)
            wait("tensor", v_c[0], v_c[1])
            guard_bank("tensor", PCn)
            te_c = mm16(PCm, slot_c, XLN[:, :], S_TE)
            slab_consumed(i_c, S_TE, te_c)
            ef0 = bvel(2 * A_EF + og * EPG, T // 2)
            of0 = bvel(2 * O_FA + gi * S2, T // 2)
            wait("scalar", S_TE, te_f)
            if gi == 0 and ch >= 1:
                wait("scalar", S_VE, up_done[ch - 1])
            op("scalar", lambda e, P=PF, d=ef0: e.activation(d, P[:, 0::2], ACT.Sigmoid))
            fc_evac_f[og] = op("scalar", lambda e, P=PF, d=of0: e.activation(
                d, P[:, 1::2], ACT.Sigmoid), S_SC)
            set_bank_guard(PFn, S_SC, fc_evac_f[og])
            wait("scalar", S_TE, te_c)
            if og == 0:
                wait("scalar", S_VE, v_xln)
            else:
                wait("scalar", S_VE, fc_evac_c[og - 1])
            sc_sg = op("scalar", lambda e, P=PCm: e.activation(sgT, P[:, :], ACT.Sigmoid), S_SC)
            ec0 = bvel(2 * A_EC + og * EPG, T // 2)
            oc0 = bvel(2 * O_CA + gi * S2, T // 2)
            wait("vector", S_SC, sc_sg)
            op("vector", lambda e, P=PCm, d=ec0: e.tensor_tensor(d, P[:, 0::2], sgT[:, 0::2], ALU.mult))
            fc_evac_c[og] = op("vector", lambda e, P=PCm, d=oc0: e.tensor_tensor(
                d, P[:, 1::2], sgT[:, 1::2], ALU.mult), S_VE)
            set_bank_guard(PCn, S_VE, fc_evac_c[og])
        EfL = ar_bf[:, 2 * A_EF + ch * GCH * EPG: 2 * A_EF + (ch + 1) * GCH * EPG].rearrange(
            "p (g x) -> p g x", g=GCH)
        EcL = ar_bf[:, 2 * A_EC + ch * GCH * EPG: 2 * A_EC + (ch + 1) * GCH * EPG].rearrange(
            "p (g x) -> p g x", g=GCH)
        m = None
        for j in range(LVL):
            npair = T >> (j + 1)
            if j % 2 == 0:
                OFI, OCI, OFO, OCO, szi, szo = O_FA, O_CA, O_FB, O_CB, S2, S2 // 2
            else:
                OFI, OCI, OFO, OCO, szi, szo = O_FB, O_CB, O_FA, O_CA, S2 // 2, S2
            OFi = ar_bf[:, 2 * OFI:2 * OFI + GCH * szi].rearrange("p (g x) -> p g x", g=GCH)
            OCi = ar_bf[:, 2 * OCI:2 * OCI + GCH * szi].rearrange("p (g x) -> p g x", g=GCH)
            OFo = ar_bf[:, 2 * OFO:2 * OFO + GCH * szo].rearrange("p (g x) -> p g x", g=GCH)
            OCo = ar_bf[:, 2 * OCO:2 * OCO + GCH * szo].rearrange("p (g x) -> p g x", g=GCH)
            if npair == 1:
                rf = t_rootf[:, ch * GCH:(ch + 1) * GCH]
                rc = t_rootc[:, ch * GCH:(ch + 1) * GCH]
                eS = EfL[:, :, EOFF[j]:EOFF[j] + 1].squeeze(2)
                cS = EcL[:, :, EOFF[j]:EOFF[j] + 1].squeeze(2)
                oS = OFi[:, :, 0:1].squeeze(2)
                ocS = OCi[:, :, 0:1].squeeze(2)
                op("vector", lambda e, a=rc, x=cS, y=ocS: e.tensor_tensor(a, x, y, ALU.subtract))
                op("vector", lambda e, a=rf, x=eS, y=oS: e.tensor_tensor(a, x, y, ALU.mult))
                op("vector", lambda e, a=rc, y=oS: e.tensor_tensor(a, a, y, ALU.mult))
                op("vector", lambda e: e.tensor_copy(t_bnd[:, 0:64], t_bnd[:, 0:64]))
                m = op("vector", lambda e, a=rc, y=ocS: e.tensor_tensor(a, a, y, ALU.add), S_VE)
                break
            nE = npair // 2
            halves = []
            for off0 in (0, 1):
                sfE = EfL[:, :, EOFF[j] + off0:EOFF[j] + npair:2]
                scE = EcL[:, :, EOFF[j] + off0:EOFF[j] + npair:2]
                sfO = OFi[:, :, off0:npair:2]
                scO = OCi[:, :, off0:npair:2]
                if off0 == 0:
                    df = EfL[:, :, EOFF[j + 1]:EOFF[j + 1] + nE]
                    dc = EcL[:, :, EOFF[j + 1]:EOFF[j + 1] + nE]
                else:
                    df = OFo[:, :, 0:nE]
                    dc = OCo[:, :, 0:nE]
                halves.append((sfE, scE, sfO, scO, df, dc))
            for h in halves:
                op("vector", lambda e, d=h[5], x=h[1], y=h[3]: e.tensor_tensor(d, x, y, ALU.subtract))
            for h in halves:
                op("vector", lambda e, d=h[4], x=h[0], y=h[2]: e.tensor_tensor(d, x, y, ALU.mult))
            for h in halves:
                op("vector", lambda e, d=h[5], y=h[2]: e.tensor_tensor(d, d, y, ALU.mult))
            for h in halves:
                op("vector", lambda e, d=h[5], y=h[3]: e.tensor_tensor(d, d, y, ALU.add))
        up_done[ch] = m

    UP_ALL = up_done[NCH - 1]

    pre_r = {}
    for og in (0, 1):
        pre_r[og] = emit_slab(d_wr[og, :, :])
    wait("gpsimd", S_VE, UP_ALL)
    dma("gpsimd", d_ccin[:, :], t_rootc[:, :], S_BC)
    wait("gpsimd", S_BC, S_BC.n)
    cc_v = op("gpsimd", lambda e: e.collective_compute(
        "AllGather", ALU.bypass, replica_groups=[[0, 1], [2, 3], [4, 5], [6, 7]],
        ins=[d_ccin.ap().opt()], outs=[d_ccout.ap().opt()]), S_CC)
    wait("gpsimd", S_CC, cc_v)
    v_K = dma("gpsimd", t_K[:, :], d_ccout[0:128, :], S_BC)

    Hb = 2 * A_H
    wait("vector", S_BC, v_K)
    wait("vector", S_BC, v_gate)
    bnds = []
    for j in range(LVL + 1):
        if j < LVL:
            f_ap = ar_bf[:, 2 * A_EF:2 * A_EF + KG * EPG].rearrange(
                "p (g x) -> p g x", g=KG)[:, :, EOFF[j]:EOFF[j] + 1].squeeze(2)
            c_ap = ar_bf[:, 2 * A_EC:2 * A_EC + KG * EPG].rearrange(
                "p (g x) -> p g x", g=KG)[:, :, EOFF[j]:EOFF[j] + 1].squeeze(2)
            pos = (1 << j) - 1
        else:
            f_ap = t_rootf[:, :]
            c_ap = t_rootc[:, :]
            pos = T - 1
        h_ap = ar_bf[:, Hb:Hb + KG * T].rearrange("p (g x) -> p g x", g=KG)[:, :, pos:pos + 1].squeeze(2)
        bnds.append((t_bnd[:, j * KG:(j + 1) * KG], f_ap, c_ap, h_ap))
    for bj, f_ap, c_ap, h_ap in bnds:
        op("vector", lambda e, b_=bj, c=c_ap: e.tensor_tensor(b_, t_K[:, :], c, ALU.subtract))
    for bj, f_ap, c_ap, h_ap in bnds:
        op("vector", lambda e, b_=bj, f=f_ap: e.tensor_tensor(b_, b_, f, ALU.mult))
    for bj, f_ap, c_ap, h_ap in bnds:
        op("vector", lambda e, b_=bj: e.tensor_tensor(b_, b_, t_gate[:, :], ALU.mult))
    BND_DONE = None
    for bj, f_ap, c_ap, h_ap in bnds:
        BND_DONE = op("vector", lambda e, b_=bj, h=h_ap, c=c_ap: e.tensor_tensor(h, b_, c, ALU.add),
                      S_VE if bj is bnds[-1][0] else None)

    te_r = {}
    sc_r = {}
    hsq_ve = {}
    sq_te = {}
    te_sq_last = None
    for ch in range(NCH):
        for gi in range(GCH):
            og = ch * GCH + gi
            par = og % 2
            P_ = PA if par == 0 else PC
            Pn = "PA" if par == 0 else "PC"
            if og in pre_r:
                i_r, slot_r, v_r = pre_r[og]
            else:
                i_r, slot_r, v_r = emit_slab(d_wr[og, :, :])
            wait("tensor", v_r[0], v_r[1])
            guard_bank("tensor", Pn)
            m = mm16(P_, slot_r, XLN[:, :], S_TE)
            te_r[og] = m
            slab_consumed(i_r, S_TE, m)
            rv = bvel(2 * A_R + og * T, T)
            wait("scalar", S_TE, m)
            sc_r[og] = op("scalar", lambda e, P=P_, d=rv: e.activation(d, P[:, :], ACT.Silu), S_SC)
            set_bank_guard(Pn, S_SC, sc_r[og])
        if ch == 0:
            wait("vector", S_VE, BND_DONE)
            for j in range(LVL, 0, -1):
                cnt = (T >> j) - 1
                if cnt <= 0:
                    continue
                step = 1 << j
                half = 1 << (j - 1)
                views = []
                for ch2 in range(NCH):
                    HbL = ar_bf[:, Hb + ch2 * GCH * T: Hb + (ch2 + 1) * GCH * T].rearrange(
                        "p (g x) -> p g x", g=GCH)
                    EfL2 = ar_bf[:, 2 * A_EF + ch2 * GCH * EPG: 2 * A_EF + (ch2 + 1) * GCH * EPG].rearrange(
                        "p (g x) -> p g x", g=GCH)
                    EcL2 = ar_bf[:, 2 * A_EC + ch2 * GCH * EPG: 2 * A_EC + (ch2 + 1) * GCH * EPG].rearrange(
                        "p (g x) -> p g x", g=GCH)
                    P_ap = HbL[:, :, step - 1:step - 1 + (cnt - 1) * step + 1:step]
                    O_ap = HbL[:, :, step + half - 1:step + half - 1 + (cnt - 1) * step + 1:step]
                    Ec_ap = EcL2[:, :, EOFF[j - 1] + 1:EOFF[j - 1] + 1 + cnt]
                    Ef_ap = EfL2[:, :, EOFF[j - 1] + 1:EOFF[j - 1] + 1 + cnt]
                    views.append((P_ap, O_ap, Ec_ap, Ef_ap))
                for P_ap, O_ap, Ec_ap, Ef_ap in views:
                    op("vector", lambda e, o=O_ap, p=P_ap, c=Ec_ap: e.tensor_tensor(o, p, c, ALU.subtract))
                for P_ap, O_ap, Ec_ap, Ef_ap in views:
                    op("vector", lambda e, o=O_ap, f=Ef_ap: e.tensor_tensor(o, o, f, ALU.mult))
                for P_ap, O_ap, Ec_ap, Ef_ap in views:
                    op("vector", lambda e, o=O_ap, c=Ec_ap: e.tensor_tensor(o, o, c, ALU.add))
        for gi in range(GCH):
            og = ch * GCH + gi
            slot = HSQ_0 if og % 2 == 0 else HSQ_1
            hq = bview(slot, T)
            hog = ar_bf[:, Hb + og * T:Hb + (og + 1) * T]
            if og >= 2:
                wait("vector", S_TE, sq_te[og - 2])
            hsq_ve[og] = op("vector", lambda e, d=hq, h=hog: e.tensor_tensor(d, h, h, ALU.mult), S_VE)
            wait("tensor", S_VE, hsq_ve[og])
            if og == 0:
                guard_bank("tensor", "PB")
            for s in range(2):
                last = (s == 1)
                te_sq_last = op("tensor", lambda e, q=hq, s=s, og=og: e.matmul(
                    PB[0:1, s * S2:(s + 1) * S2], t_onesb[:, :], q[:, s * S2:(s + 1) * S2],
                    start=(og == 0), stop=(og == KG - 1)), S_TE if last else None)
            sq_te[og] = te_sq_last

    if debug_stage == "h":
        wait("gpsimd", S_VE, S_VE.n)
        for g in range(KG):
            dma("gpsimd", d_dbg[g * 128:(g + 1) * 128, :], ar_bf[:, Hb + g * T:Hb + (g + 1) * T], S_BC)

    wait("scalar", S_TE, te_sq_last)
    op("scalar", lambda e: e.mul(rt, PB[0:1, :], t_cst[0:1, 4:5]))
    op("scalar", lambda e: e.activation(rt, rt, ACT.Ln, bias=t_cst[0:1, 1:2]))
    sc_rows2 = op("scalar", lambda e: e.activation(rt, rt, ACT.Exp, scale=t_cst[0:1, 2:3]), S_SC)
    set_bank_guard("PB", S_SC, sc_rows2)
    wait("gpsimd", S_SC, sc_rows2)
    g_bc2 = emit_bcast(False)

    if debug_stage == "h":
        wait("vector", S_BC, S_BC.n)

    mov_done = None
    wait("vector", S_BC, g_bc2)
    wait("vector", S_BC, v_lns)
    hn = fv(A_SG, T)
    for og in range(KG):
        hog = ar_bf[:, Hb + og * T:Hb + (og + 1) * T]
        rv = bvel(2 * A_R + og * T, T)
        wait("vector", S_SC, sc_r[og])
        op("vector", lambda e, h=hog: e.tensor_tensor(hn, h, bc_a, ALU.mult))
        op("vector", lambda e, h=hog, r=rv: e.tensor_tensor(h, hn, r, ALU.mult))
        mov_done = op("vector", lambda e, h=hog, og=og: e.tensor_scalar(
            h, h, t_lns[:, og:og + 1], None, ALU.mult), S_VE if og == KG - 1 else None)
    MOV = mov_done

    if debug_stage == "mov":
        wait("gpsimd", S_VE, MOV)
        for g in range(KG):
            dma("gpsimd", d_dbg[g * 128:(g + 1) * 128, :], ar_bf[:, Hb + g * T:Hb + (g + 1) * T], S_BC)
        wait("tensor", S_BC, S_BC.n)

    yevac = {}
    te_y = {}
    te_ln2_last = None
    v_x1out = {}
    movv = ar_bf[:, Hb:Hb + KG * T]
    for og in range(KG):
        par = og % 2
        P_ = PA if par == 0 else PC
        Pn = "PA" if par == 0 else "PC"
        i_o, slot_o, v_o = emit_slab(d_wo[og, :, :])
        wait("tensor", v_o[0], v_o[1])
        if og == 0:
            wait("tensor", S_VE, MOV)
        guard_bank("tensor", Pn)
        m = mm16(P_, slot_o, movv, S_TE)
        te_y[og] = m
        slab_consumed(i_o, S_TE, m)
        xr = fv(XR_ST + (og % 2) * 1024, T)
        x1s = fv(X1_ST + (og % 2) * 1024, T)
        if og == 0:
            wait("sync", S_VE, MOV)
        i_xr, xrs, xrv = R_XR.issue("sync", xr, d_x[og * 128:(og + 1) * 128, :])
        wait("vector", S_TE, te_y[og])
        wait("vector", xrs, xrv)
        if og >= 2:
            g = R_X1O.guards[og - 2] if False else None
            s2_, v2_ = v_x1out[og - 2]
            wait("vector", s2_, v2_)
        op("vector", lambda e, P=P_, x=xr, d=x1s: e.tensor_tensor(d, P[:, :], x, ALU.add))
        xb1 = bview(XB1_0 if og % 2 == 0 else XB1_1, T)
        xq1 = bview(XQ1_0 if og % 2 == 0 else XQ1_1, T)
        op("vector", lambda e, d=xb1, s=x1s: e.tensor_copy(d, s))
        yevac[og] = op("vector", lambda e, d=xq1, s=xb1: e.tensor_tensor(d, s, s, ALU.mult), S_VE)
        set_bank_guard(Pn, S_VE, yevac[og])
        R_XR.consumed(i_xr, S_VE, yevac[og])
        wait("sync", S_VE, yevac[og])
        i_xo, xos, xov = R_X1O.issue("sync", d_x1[og * 128:(og + 1) * 128, :], x1s)
        v_x1out[og] = (xos, xov)
        R_X1O.consumed(i_xo, S_VE, yevac[og])
        wait("tensor", S_VE, yevac[og])
        if og == 0:
            guard_bank("tensor", "PB")
            guard_bank("tensor", "PD")
        for s in range(2):
            op("tensor", lambda e, q=xb1, s=s, og=og: e.matmul(
                PB[0:1, s * S2:(s + 1) * S2], t_onesb[:, :], q[:, s * S2:(s + 1) * S2],
                start=(og == 0), stop=(og == KG - 1)))
        for s in range(2):
            last = (s == 1)
            te_ln2_last = op("tensor", lambda e, q=xq1, s=s, og=og: e.matmul(
                PD[0:1, s * S2:(s + 1) * S2], t_onesb[:, :], q[:, s * S2:(s + 1) * S2],
                start=(og == 0), stop=(og == KG - 1)), S_TE if last else None)

    ve_var2, ve_rows2b = emit_rows(PB[0:1, :], PD[0:1, :], te_ln2_last)
    set_bank_guard("PB", S_VE, ve_var2)
    set_bank_guard("PD", S_VE, ve_var2)
    wait("gpsimd", S_VE, ve_rows2b)
    g_bc3 = emit_bcast(True)

    if debug_stage == "x1":
        R_X1O.drain("gpsimd")
        dbg_slot = fv(8192, T)
        for og in range(KG):
            dma("gpsimd", dbg_slot, d_x1[og * 128:(og + 1) * 128, :], S_BC)
            wait("gpsimd", S_BC, S_BC.n)
            dma("gpsimd", d_dbg[og * 128:(og + 1) * 128, :], dbg_slot, S_BC)
            wait("gpsimd", S_BC, S_BC.n)
        wait("sync", S_BC, S_BC.n)

    b0_done = {}
    for og in range(KG):
        slot = fv(B_SG + (og % 2) * 1024, T)
        if og == 0:
            wait("sync", S_TE, te_ln2_last)
            R_X1O.drain("sync")
        i_b0, b0s, b0v = R_B0.issue("sync", slot, d_x1[og * 128:(og + 1) * 128, :])
        wait("vector", b0s, b0v)
        if og == 0:
            wait("vector", S_BC, g_bc3)
            wait("vector", S_TE, te_r[KG - 1])
        bsg = fv(A_SG, T)
        op("vector", lambda e, s=slot, d=bsg: e.tensor_tensor(d, s, bc_a, ALU.mult))
        b0_done[og] = op("vector", lambda e, og=og, d=bsg: e.tensor_tensor(
            XLN[:, og * T:(og + 1) * T], d, bc_b, ALU.subtract), S_VE)
        R_B0.consumed(i_b0, S_VE, b0_done[og])
    B0_ALL = b0_done[KG - 1]

    if debug_stage == "xln2":
        wait("gpsimd", S_VE, B0_ALL)
        for g in range(KG):
            dma("gpsimd", d_dbg[g * 128:(g + 1) * 128, :], XLN[:, g * T:(g + 1) * T], S_BC)
        wait("tensor", S_BC, S_BC.n)

    for og in range(KG):
        for q in range(8):
            emit_p1_piece(d_w2[og, :, q * 688:(q + 1) * 688], 688, col)
            col += 1
    P1_W2_LAST = p1_done[R_P1.count - 1]
    FS1 = emit_s_finalize(0, 86, 0, 0, P1_W1_LAST)
    FS3 = emit_s_finalize(86, 172, 1, 1, P1_W3_LAST)
    FS2 = emit_s_finalize(172, 300, 2, 2, P1_W2_LAST)
    op("vector", lambda e: e.tensor_tensor(t_s[:, 6:7], t_s[:, 0:1], t_s[:, 1:2], ALU.mult))
    op("vector", lambda e: e.tensor_copy(t_bnd[:, 0:64], t_bnd[:, 0:64]))
    SALL = op("vector", lambda e: e.tensor_tensor(t_s[:, 6:7], t_s[:, 6:7], t_s[:, 2:3], ALU.mult), S_VE)

    if debug_stage == "s":
        wait("gpsimd", S_VE, SALL)
        dma("gpsimd", d_dbg[0:128, 0:8], t_s[:, :], S_BC)
        dma("gpsimd", d_dbg[128:256, 0:308], t_acc[:, :], S_BC)
        wait("tensor", S_BC, S_BC.n)

    q_ve = {}
    sgn_sc = {}
    g_ve = {}
    te_h = {}
    for hg in range(HG):
        par = hg % 2
        P1_ = PA if par == 0 else PC
        P1n = "PA" if par == 0 else "PC"
        P3_ = PB if par == 0 else PD
        P3n = "PB" if par == 0 else "PD"
        w1sl = fv(B_W1 + (hg % 2) * 2048, 2048)
        w3sl = fv(B_W3 + (hg % 2) * 2048, 2048)
        i1, s1sem, v1 = R_W1.issue("sync", w1sl, d_w1[hg, :, :])
        i3, s3sem, v3 = R_W3.issue("sync", w3sl, d_w3[hg, :, :])
        qa = bview(B_QA, 2048)
        qb = bview(B_QB, 2048)
        t1b = bview(B_T1, 2048)
        t3b = bview(B_T3, 2048)
        wait("scalar", s1sem, v1)
        if hg == 0:
            wait("scalar", S_VE, FS1)
            wait("scalar", S_VE, FS3)
        else:
            wait("scalar", S_VE, q_ve[(hg - 1, 3)])
        op("scalar", lambda e, w=w1sl: e.sign(qa, w, bias=t_s[:, 3:4]))
        s1s = op("scalar", lambda e, w=w1sl: e.sign(qb, w, bias=t_s[:, 0:1]), S_SC)
        wait("vector", S_SC, s1s)
        if hg >= 1:
            wait("vector", S_TE, te_h[hg - 1])
        q_ve[(hg, 1)] = op("vector", lambda e, a=qa, b=qb, d=t1b: e.tensor_tensor(d, a, b, ALU.add), S_VE)
        wait("scalar", s3sem, v3)
        wait("scalar", S_VE, q_ve[(hg, 1)])
        op("scalar", lambda e, w=w3sl: e.sign(qa, w, bias=t_s[:, 4:5]))
        s3s = op("scalar", lambda e, w=w3sl: e.sign(qb, w, bias=t_s[:, 1:2]), S_SC)
        sgn_sc[hg] = s3s
        R_W1.consumed(i1, S_SC, s3s)
        R_W3.consumed(i3, S_SC, s3s)
        wait("vector", S_SC, s3s)
        q_ve[(hg, 3)] = op("vector", lambda e, a=qa, b=qb, d=t3b: e.tensor_tensor(d, a, b, ALU.add), S_VE)
        wait("tensor", S_VE, q_ve[(hg, 3)])
        if hg == 0:
            wait("tensor", S_VE, B0_ALL)
        guard_bank("tensor", P1n)
        guard_bank("tensor", P3n)
        mm16(P1_, t1b, XLN[:, :], None)
        m = mm16(P3_, t3b, XLN[:, :], S_TE)
        te_h[hg] = m
        bsg = fv(B_SG, T)
        bc1v = fv(B_C1, T)
        gv = bvel(2 * B_G + hg * T, T)
        wait("scalar", S_TE, m)
        if hg == 0:
            wait("scalar", S_VE, B0_ALL)
        sgv = op("scalar", lambda e, P=P1_: e.activation(bsg, P[:, :], ACT.Sigmoid,
                                                         scale=t_s[:, 0:1]), S_SC)
        wait("vector", S_SC, sgv)
        op("vector", lambda e, P=P1_: e.tensor_tensor(bc1v, P[:, :], bsg, ALU.mult))
        g_ve[hg] = op("vector", lambda e, P=P3_, d=gv: e.tensor_tensor(d, bc1v, P[:, :], ALU.mult), S_VE)
        set_bank_guard(P1n, S_VE, g_ve[hg])
        set_bank_guard(P3n, S_VE, g_ve[hg])

    if debug_stage == "g":
        wait("gpsimd", S_VE, g_ve[HG - 1])
        for kh in range(KG):
            dma("gpsimd", d_dbg[kh * 128:(kh + 1) * 128, :],
                ar_bf[:, 2 * B_G + kh * T:2 * B_G + (kh + 1) * T], S_BC)
        wait("tensor", S_BC, S_BC.n)

    oevac = {}
    v_odma = {}
    sgn_sc2_last = None
    te_o_last = None
    gall = ar_bf[:, 2 * B_G:2 * B_G + HG * T]
    for og in range(KG):
        par = og % 2
        P_ = PA if par == 0 else PC
        Pn = "PA" if par == 0 else "PC"
        w2sl = fv(B_W2, HP)
        if og == 0:
            wait("sync", S_SC, sgn_sc[HG - 1])
            wait("sync", S_VE, g_ve[HG - 1])
        i2s, w2sem, v2s = R_W2.issue("sync", w2sl, d_w2[og, :, :])
        t2b = bview(B_T2 + (og % 2) * 2752, HP)
        qw = bview(B_QW, 2752)
        wait("scalar", w2sem, v2s)
        if og == 0:
            wait("scalar", S_VE, FS2)
            wait("scalar", S_TE, te_h[HG - 1])
            wait("scalar", S_VE, g_ve[HG - 1])
        if og >= 2:
            wait("scalar", S_TE, te_o_last)
        op("scalar", lambda e, w=w2sl, d=t2b: e.sign(d, w, bias=t_s[:, 5:6]))
        h1v = op("scalar", lambda e, w=w2sl, q=qw: e.sign(q, w[:, 0:2752], bias=t_s[:, 2:3]), S_SC)
        wait("vector", S_SC, h1v)
        op("vector", lambda e, d=t2b, q=qw: e.tensor_tensor(d[:, 0:2752], d[:, 0:2752], q, ALU.add))
        h2v = op("scalar", lambda e, w=w2sl, q=qw: e.sign(q, w[:, 2752:HP], bias=t_s[:, 2:3]), S_SC)
        sgn_sc2_last = h2v
        R_W2.consumed(i2s, S_SC, h2v)
        wait("vector", S_SC, h2v)
        q2v = op("vector", lambda e, d=t2b, q=qw: e.tensor_tensor(
            d[:, 2752:HP], d[:, 2752:HP], q, ALU.add), S_VE)
        wait("tensor", S_VE, q2v)
        if og == 0:
            wait("tensor", S_VE, g_ve[HG - 1])
        guard_bank("tensor", Pn)
        m = None
        for kh in range(HG):
            last = (kh == HG - 1)
            m = op("tensor", lambda e, kh=kh, P=P_, sl=t2b: (
                e.matmul(P[:, 0:S2], sl[:, kh * 128:(kh + 1) * 128],
                         gall[:, kh * T:kh * T + S2], start=(kh == 0), stop=(kh == HG - 1)),
                e.matmul(P[:, S2:T], sl[:, kh * 128:(kh + 1) * 128],
                         gall[:, kh * T + S2:(kh + 1) * T], start=(kh == 0), stop=(kh == HG - 1)))[1],
                S_TE if last else None)
        te_o_last = m
        x1r = fv(B_X1R, T)
        outs = fv(B_OUT, T)
        ixr2, xr2sem, v_x1r = R_X1R.issue("sync", x1r, d_x1[og * 128:(og + 1) * 128, :])
        wait("vector", S_TE, m)
        wait("vector", xr2sem, v_x1r)
        if og == 0:
            wait("vector", S_VE, SALL)
        if og >= 1:
            wait("vector", S_ST, v_odma[og - 1])
        oevac[og] = op("vector", lambda e, P=P_, x=x1r, d=outs: e.scalar_tensor_tensor(
            d, P[:, :], t_s[:, 6:7], x, ALU.mult, ALU.add), S_VE)
        set_bank_guard(Pn, S_VE, oevac[og])
        R_X1R.consumed(ixr2, S_VE, oevac[og])
        wait("sync", S_VE, oevac[og])
        v_odma[og] = dma("sync", d_out[og * 128:(og + 1) * 128, :], outs, S_ST)

    wait("sync", S_ST, S_ST.n)

    with nc.Block() as block:
        @block.sync
        def _(e):
            for f in ops["sync"]:
                f(e)

        @block.gpsimd
        def _(e):
            for f in ops["gpsimd"]:
                f(e)

        @block.vector
        def _(e):
            for f in ops["vector"]:
                f(e)

        @block.scalar
        def _(e):
            for f in ops["scalar"]:
                f(e)

        @block.tensor
        def _(e):
            for f in ops["tensor"]:
                f(e)

    stack.close()
    return nc


_NC_CACHE = {}


def _get_nc(debug_stage=None):
    if debug_stage not in _NC_CACHE:
        _NC_CACHE[debug_stage] = build_nc(debug_stage)
    return _NC_CACHE[debug_stage]


def stage_inputs(L0_attn_x, L0_attn_Wr, L0_attn_Wf, L0_attn_Wc, L0_attn_Wo,
                 L0_attn_ln_scale, L0_ffn_w1, L0_ffn_w3, L0_ffn_w2):
    x = np.asarray(L0_attn_x, dtype=np.float32)

    def attn_slabs(W):
        W = np.asarray(W, dtype=np.float32)
        return np.ascontiguousarray(W.reshape(KG, 128, KG, 128).transpose(2, 1, 0, 3)
                                    .reshape(KG, 128, D))

    wrs = attn_slabs(L0_attn_Wr)
    wfs = attn_slabs(L0_attn_Wf)
    wcs = attn_slabs(L0_attn_Wc)
    wos = attn_slabs(L0_attn_Wo)

    def w13_slabs(W):
        W = np.asarray(W, dtype=np.float32)
        wt = np.zeros((D, HP), np.float32)
        wt[:, :H_REAL] = W.T
        return np.ascontiguousarray(wt.reshape(KG, 128, HG, 128).transpose(2, 1, 0, 3)
                                    .reshape(HG, 128, D))

    w1s = w13_slabs(L0_ffn_w1)
    w3s = w13_slabs(L0_ffn_w3)
    w2t = np.zeros((HP, D), np.float32)
    w2t[:H_REAL, :] = np.asarray(L0_ffn_w2, np.float32).T
    w2s = np.ascontiguousarray(w2t.reshape(HG, 128, KG, 128).transpose(2, 1, 0, 3)
                               .reshape(KG, 128, HP))
    lns = np.ascontiguousarray(np.asarray(L0_attn_ln_scale, np.float32).reshape(KG, 128).T)
    onesb = np.ones((128, 1), ml_dtypes.bfloat16)

    in_maps = []
    for c in range(8):
        b, h = c // 2, c % 2
        xT = np.ascontiguousarray(x[b, h * T:(h + 1) * T, :].T)
        gates = np.full((128, KG), float(h), np.float32)
        in_maps.append({
            "xT": xT, "wrs": wrs, "wfs": wfs, "wcs": wcs, "wos": wos,
            "w1s": w1s, "w3s": w3s, "w2s": w2s, "lns": lns,
            "gates": gates, "onesb": onesb,
        })
    return in_maps


def kernel(L0_attn_x, L0_attn_Wr, L0_attn_Wf, L0_attn_Wc, L0_attn_Wo,
           L0_attn_ln_scale, L0_ffn_w1, L0_ffn_w3, L0_ffn_w2):
    in_maps = stage_inputs(L0_attn_x, L0_attn_Wr, L0_attn_Wf, L0_attn_Wc,
                           L0_attn_Wo, L0_attn_ln_scale, L0_ffn_w1, L0_ffn_w3,
                           L0_ffn_w2)
    nc = _get_nc()
    res = run_bass_kernel_spmd(nc, in_maps, core_ids=list(range(8)))
    out = np.empty((4, 2048, D), np.float32)
    for c in range(8):
        b, h = c // 2, c % 2
        out[b, h * T:(h + 1) * T, :] = res.results[c]["outT"].T
    return out
